# version 09
# speedup vs baseline: 1.8044x; 1.0657x over previous
import os
import numpy as np
import ml_dtypes
import concourse.bass as bass
import concourse.mybir as mybir
from concourse.bass_utils import run_bass_kernel_spmd

F32 = mybir.dt.float32
BF16 = mybir.dt.bfloat16
I32 = mybir.dt.int32
AF = mybir.ActivationFunctionType
ALU = mybir.AluOpType
AX = mybir.AxisListType
BF = ml_dtypes.bfloat16

T = 2048
D = 1024
NTT = 16
NDC = 8
I0 = 2816
NIC = 22
NH = 16
HS = 64
E_MOE = 4
HID = 3072
NHC = 24
EPS_LN = 1e-5
EPS_RMS = 1.1920928955078125e-07
QMAX = 127.0

DEBUG = os.environ.get("KD_DEBUG", "0") == "1"
RACE_SAFE_MODE = os.environ.get("KD_RACE_SAFE", "notensor")
RACE_SAFE = RACE_SAFE_MODE != "0"
PHASE_LIMIT = int(os.environ.get("KD_PHASES", "999"))

ENGINES = ("sync", "tensor", "vector", "scalar", "gpsimd")


class Em:

    def __init__(self, nc, stack):
        self.nc = nc
        self.ops = {e: [] for e in ENGINES}
        self.sems = {}
        self.cnt = {}
        self.stack = stack
        for name in ("dma", "dmg", "te", "ve", "se", "ge", "bar", "cc", "xd",
                     "dm_c", "dm_x", "dm_w", "dm_s", "dm_h", "dm_t", "dm_o",
                     "dm_g1", "dm_g2"):
            self.sems[name] = stack.enter_context(nc.semaphore(name))
            self.cnt[name] = 0
        self.barn = 0
        self.bank_guard = {}
        self.phase_no = 0

    ENG_SEM = {"tensor": "te", "vector": "ve", "scalar": "se", "gpsimd": "ge"}

    def do(self, eng, fn, waits=(), sem=None, inc=0):
        waits = list(waits)
        if RACE_SAFE and eng in self.ENG_SEM and not (
                RACE_SAFE_MODE == "notensor" and eng == "tensor"):
            es = self.ENG_SEM[eng]
            waits.append((es, self.cnt[es]))
        w = [(self.sems[s], v) for s, v in waits if v > 0]
        if sem is None:
            self.ops[eng].append((fn, w, None, 0))
            return None
        self.cnt[sem] += inc
        self.ops[eng].append((fn, w, self.sems[sem], inc))
        return self.cnt[sem]

    def dma(self, eng, out, in_, waits=(), ch="c", force_ch=False):
        sem = ("dm_" + ch) if force_ch else ("dmg" if eng == "gpsimd" else ("dm_" + ch))
        waits = list(waits) + [(sem, self.cnt[sem])]
        return self.do(eng, lambda e, out=out, in_=in_: e.dma_start(out=out, in_=in_),
                       waits=waits, sem=sem, inc=16)

    def barrier(self):
        self.barn += 1
        tgt = 4 * self.barn
        self.do("sync", lambda eng: eng.nop(),
                waits=tuple((s, self.cnt[s]) for s in self.cnt
                            if (s.startswith("dm_") or s == "dmg") and self.cnt[s] > 0),
                sem="bar", inc=1)
        for e in ("tensor", "vector", "scalar"):
            self.do(e, lambda eng: eng.nop(), sem="bar", inc=1)
        assert self.cnt["bar"] == tgt
        for e in ("sync", "tensor", "vector", "scalar"):
            self.do(e, lambda eng: eng.nop(), waits=(("bar", tgt),))
        self.bank_guard = {}

    def emit(self):
        blk = self.stack.enter_context(self.nc.Block())

        def mk(eng_name):
            def body(e):
                for fn, waits, sem, inc in self.ops[eng_name]:
                    for s, v in waits:
                        e.wait_ge(s, int(v))
                    inst = fn(e)
                    if sem is not None:
                        inst.then_inc(sem, int(inc))
            return body

        cd = getattr(self, "const_done", 0)
        if cd:
            for e in ("tensor", "vector", "scalar", "gpsimd"):
                self.ops[e].insert(0, (lambda eng: eng.nop(), [(self.sems["dm_c"], cd)], None, 0))
        blk.sync(mk("sync"))
        blk.tensor(mk("tensor"))
        blk.vector(mk("vector"))
        blk.scalar(mk("scalar"))
        blk.gpsimd(mk("gpsimd"))


def _bank(ps, i, w=512):
    return ps[:, i * 512:i * 512 + w]


def host_constants():
    c = {}
    c["c_ident"] = np.eye(128, dtype=np.float32)
    c["c_identbf"] = np.eye(128).astype(BF)
    md = np.where(np.tril(np.ones((128, 128), bool)), 0.0, -1e30).astype(np.float32)
    c["c_maskdiag"] = md.astype(BF)
    c["c_maskdiag32"] = md
    c["c_ones"] = np.ones((1, 128), dtype=np.float32)
    elo = np.zeros((1, 128), dtype=np.float32); elo[0, :64] = 1.0
    ehi = np.zeros((1, 128), dtype=np.float32); ehi[0, 64:] = 1.0
    c["c_elo"] = elo; c["c_ehi"] = ehi
    inv = 1.0 / 10000.0 ** (np.arange(0, 64, 2, dtype=np.float64) / 64.0)
    fr = np.arange(T, dtype=np.float64)[:, None] * inv[None, :]
    emb = np.concatenate([fr, fr], axis=-1)
    cos = np.cos(emb); sin = np.sin(emb)
    sin_signed = sin.copy()
    sin_signed[:, :32] *= -1.0
    c["c_cos"] = cos.reshape(NTT, 128, 64).transpose(1, 0, 2).reshape(128, NTT * 64).astype(np.float32)
    c["c_sin"] = sin_signed.reshape(NTT, 128, 64).transpose(1, 0, 2).reshape(128, NTT * 64).astype(np.float32)
    xsl = (2.0 ** 8) ** (1.0 / NH)
    slopes = np.array([1.0 / xsl ** (i + 1) for i in range(NH)], dtype=np.float64)
    k = np.arange(T, dtype=np.float64)
    f = np.exp(slopes[None, :] * (k[:, None] - (T - 1)))
    c["c_ftab"] = f.reshape(NTT, 128, NH).transpose(1, 0, 2).reshape(128, NTT * NH).astype(np.float32)
    return c


def build(debug_names=()):
    import contextlib
    nc = bass.Bass("TRN2", target_bir_lowering=False,
                   detect_race_conditions=RACE_SAFE)
    stack = contextlib.ExitStack()
    em = Em(nc, stack)

    din = {}
    in_specs = [
        ("L0_attn_x", [T, D]), ("L0_attn_W_dq", [128, D]), ("L0_attn_W_uq", [D, 128]),
        ("L0_attn_W_dkv", [128, D]), ("L0_attn_W_uk", [D, 128]), ("L0_attn_W_uv", [D, 128]),
        ("L0_attn_W_o", [D, D]), ("L0_ffn_clip_x", [1, 1]), ("L0_ffn_clip_down", [1, 1]),
        ("L0_ffn_w_gate", [I0, D]), ("L0_ffn_w_up", [I0, D]), ("L0_ffn_w_down", [D, I0]),
        ("L1_attn_Wq", [D, D]), ("L1_attn_Wkv", [D, 512]), ("L1_attn_Wo", [D, D]),
        ("L1_ffn_g1", [1, D]), ("L1_ffn_g2", [1, D]),
        ("L1_ffn_wq", [D, D]), ("L1_ffn_wk", [D, 256]), ("L1_ffn_wv", [D, 256]),
        ("L1_ffn_wo", [D, D]), ("L1_ffn_gate", [D, 4]),
        ("L1_ffn_w1", [E_MOE, D, HID]), ("L1_ffn_w2", [E_MOE, HID, D]), ("L1_ffn_w3", [E_MOE, D, HID]),
    ]
    for n, s in in_specs:
        din[n] = nc.dram_tensor(n, s, F32, kind="ExternalInput").ap()
    cns = host_constants()
    for n, v in cns.items():
        dt = BF16 if v.dtype == BF else F32
        din[n] = nc.dram_tensor(n, list(v.shape), dt, kind="ExternalInput").ap()

    d_y = nc.dram_tensor("y", [T, D], F32, kind="ExternalOutput").ap()

    dsc = {}
    def scratch(name, shape, dt=F32):
        dsc[name] = nc.dram_tensor(name, list(shape), dt, kind="Internal").ap()
        return dsc[name]
    scratch("d_x", [T, D])
    scratch("d_xnT", [D, T])
    scratch("d_XintT", [D, T], BF16)
    scratch("d_WgT", [D, I0], BF16)
    scratch("d_WuT", [D, I0], BF16)
    scratch("d_WdT", [I0, D], BF16)
    scratch("d_q2T", [D, T])
    scratch("d_h", [I0, T])
    scratch("d_sT", [D, T])
    scratch("d_siT", [D, T])
    scratch("d_qT1", [D, T])
    scratch("d_kT1", [256, T])
    scratch("d_qrT", [D, T])
    scratch("d_krT", [256, T])
    scratch("d_rms1T", [D, T])
    scratch("d_rms2T", [D, T])
    scratch("d_xmid", [T, D])
    scratch("d_yT", [D, T])
    scratch("d_comb", [4, T])
    for nm in ("w1", "w2", "w3"):
        for part in ("hi", "lo"):
            scratch(f"d_{nm}{part}", [E_MOE, D if nm != "w2" else HID,
                                      HID if nm != "w2" else D], BF16)
    scratch("d_xmidT", [D, T])
    scratch("d_cc1i", [1, 1]); scratch("d_cc1o", [1, 1])
    scratch("d_cc2i", [1, 1]); scratch("d_cc2o", [1, 1])

    dbg = {}
    d_stats = None
    if os.environ.get("KD_STATS", "0") == "1":
        d_stats = nc.dram_tensor("dbg_stats", [128, 64], F32, kind="ExternalOutput").ap()
    for n in debug_names:
        src = dsc.get(n)
        if src is not None:
            dbg[n] = nc.dram_tensor("dbg_" + n, list(src.shape), src.dtype,
                                    kind="ExternalOutput").ap()

    sb = lambda n, s, dt=F32: nc.alloc_sbuf_tensor(n, s, dt)
    A = sb("arA", [128, 16384])
    Bb = sb("arB", [128, 8192])
    C = sb("arC", [128, 8192])
    Dd = sb("arD", [128, 4096])
    Ee = sb("arE", [128, 2048])
    Ff = sb("arF", [128, 2048])
    G = sb("arG", [128, 1024])
    w2scr = sb("w2scr", [128, 3072])
    cI = sb("cI", [128, 128])
    cIb = sb("cIb", [128, 128], BF16)
    cMd = sb("cMd", [128, 128], BF16)
    cMd32 = sb("cMd32", [128, 128])
    cOnes = sb("cOnes", [1, 128])
    cElo = sb("cElo", [1, 128])
    cEhi = sb("cEhi", [1, 128])
    cCos = sb("cCos", [128, NTT * 64])
    cSin = sb("cSin", [128, NTT * 64])
    cFtab = sb("cFtab", [128, NTT * NH])
    st_r0 = sb("st_r0", [128, NTT]); st_mr0 = sb("st_mr0", [128, NTT])
    st_r1 = sb("st_r1", [128, NTT]); st_mr1 = sb("st_mr1", [128, NTT])
    st_rr1 = sb("st_rr1", [128, NTT]); st_rr2 = sb("st_rr2", [128, NTT])
    sm = sb("sm", [128, 64])
    smrow = sb("smrow", [1, 128])
    cEps = sb("cEps", [128, 2])
    l0w = sb("l0w", [128, 4096])
    sgv = sb("sgv", [128, NIC]); suv = sb("suv", [128, NIC])
    sdv = sb("sdv", [128, NDC])

    ps = nc.alloc_psum_tensor("ps", [128, 4096], F32)

    for n in cns:
        tgt = {"c_ident": cI, "c_identbf": cIb, "c_maskdiag": cMd, "c_maskdiag32": cMd32,
               "c_ones": cOnes,
               "c_elo": cElo, "c_ehi": cEhi, "c_cos": cCos, "c_sin": cSin,
               "c_ftab": cFtab}[n]
        em.dma("sync", tgt[:], din[n])
    cdone = em.cnt["dm_c"]
    em.const_done = cdone
    em.do("vector", lambda e: e.memset(cEps[:, 0:1], EPS_LN), sem="ve", inc=1)
    em.do("vector", lambda e: e.memset(cEps[:, 1:2], EPS_RMS), sem="ve", inc=1)
    for _t in (sm, smrow, sgv, suv, sdv):
        em.do("vector", lambda e, _t=_t: e.memset(_t[:], 0.0), sem="ve", inc=1)

    def ve(fn, waits=()):
        return em.do("vector", fn, waits=waits, sem="ve", inc=1)

    def se(fn, waits=()):
        return em.do("scalar", fn, waits=waits, sem="se", inc=1)

    def te(fn, waits=()):
        return em.do("tensor", fn, waits=waits, sem="te", inc=1)

    def use_bank(i, waits):
        g = em.bank_guard.get(i)
        return list(waits) + ([g] if g else [])

    def read_bank(i, guard):
        em.bank_guard[i] = guard

    def quant_chain(out_bf, in_ap, rs_ap, scr1, scr2, scr3, scr_i32, w_in=(), w_rs=()):
        ve(lambda e: e.tensor_scalar_mul(scr2, in_ap, rs_ap), waits=list(w_in) + list(w_rs))
        ve(lambda e: e.tensor_copy(scr_i32, scr2))
        ve(lambda e: e.tensor_tensor(scr1, scr2, scr_i32, ALU.subtract))
        ve(lambda e: e.tensor_single_scalar(scr2, scr1, 0.5, ALU.is_ge))
        ve(lambda e: e.tensor_single_scalar(scr3, scr1, -0.5, ALU.is_lt))
        ve(lambda e: e.tensor_tensor(scr1, scr2, scr3, ALU.subtract))
        ve(lambda e: e.tensor_tensor(scr2, scr_i32, scr1, ALU.add))
        c4 = ve(lambda e: e.tensor_scalar(out_bf, scr2, -128.0, 127.0, ALU.max, ALU.min))
        return c4

    PHASE = [0]

    def phase_on():
        PHASE[0] += 1
        return PHASE[0] <= PHASE_LIMIT

    def phase_W1():
        wrow = A[:, 0:2816]
        scr1 = A[:, 2816:5632]; scr2 = A[:, 5632:8448]; scr3 = A[:, 8448:11264]
        scr_i32 = Bb[:, 0:2816].bitcast(I32)
        wint = C[:, 0:1408].bitcast(BF16)
        tout = C[:, 1408:2816].bitcast(BF16)
        rmax = G[:, 0:1]; rrec = G[:, 1:2]
        for wi, (dw, dwt, svec, nr) in enumerate([
                (din["L0_ffn_w_gate"], dsc["d_WgT"], sgv, NIC),
                (din["L0_ffn_w_up"], dsc["d_WuT"], suv, NIC),
                (din["L0_ffn_w_down"], dsc["d_WdT"], sdv, NDC)]):
            ncols = dw.shape[1]
            for rc in range(nr):
                ld = em.dma("sync", wrow[:, 0:ncols], dw[rc * 128:(rc + 1) * 128, :],
                            waits=(("ve", em.cnt["ve"]),), ch="w")
                wld = (("dm_w", ld),)
                c1 = ve(lambda e, w=wrow, nc_=ncols: e.tensor_reduce(
                    rmax, w[:, 0:nc_], AX.X, ALU.max, apply_absolute_value=True), waits=wld)
                c2 = ve(lambda e, sv=svec, rc=rc: e.tensor_scalar_mul(sv[:, rc:rc + 1], rmax, 1.0 / 127.0))
                c3 = ve(lambda e: e.reciprocal(rrec, rmax))
                c4 = ve(lambda e: e.tensor_scalar_mul(rrec, rrec, 127.0))
                cq = quant_chain(wint[:, 0:ncols], wrow[:, 0:ncols], rrec, scr1[:, 0:ncols],
                                 scr2[:, 0:ncols], scr3[:, 0:ncols], scr_i32[:, 0:ncols],
                                 w_in=(("dm_w", ld), ("te", em.cnt["te"])))
                nct = ncols // 128
                for g0 in range(0, nct, 4):
                    gn = min(4, nct - g0)
                    tcn = []
                    for j in range(g0, g0 + gn):
                        tcn.append(te(lambda e, j=j, jj=(j - g0): e.transpose(
                            ps[:, 3072 + jj * 64: 3072 + jj * 64 + 64].bitcast(BF16),
                            wint[:, j * 128:(j + 1) * 128], cIb[:]),
                            waits=use_bank(6, (("ve", cq),))))
                    cev = ve(lambda e, gn=gn, g0=g0: e.tensor_copy(
                        tout[:, g0 * 128:g0 * 128 + gn * 128],
                        ps[:, 3072:3072 + gn * 64].bitcast(BF16)),
                        waits=(("te", tcn[-1]), ("dm_s", em.cnt["dm_s"])))
                    read_bank(6, ("ve", cev))
                em.dma("sync",
                       dwt[:, rc * 128:rc * 128 + 128].rearrange("(c p) j -> p c j", p=128),
                       tout[:, 0:ncols].rearrange("p (c j) -> p c j", j=128),
                       waits=(("ve", em.cnt["ve"]),), ch="s")
        em.barrier()


    def phase_L0A(xdram, r_st, mr_st, dxnT, smax_col, do_xnT=True):
        xt = Bb[:, 0:1024]
        sq = Bb[:, 1024:2048]
        xn = Bb[:, 2048:3072]
        s_sum = G[:, 0:1]; s_ssq = G[:, 2:3]; s_mean = G[:, 3:4]
        s_var = G[:, 4:5]; s_m2 = G[:, 5:6]; s_std = G[:, 6:7]
        amax = sm[:, smax_col:smax_col + 1]
        cz = ve(lambda e: e.memset(amax, -1e30))
        for tt in range(NTT):
            ld = em.dma("sync", xt[:], xdram[tt * 128:(tt + 1) * 128, :],
                        waits=(("ve", em.cnt["ve"]), ("se", em.cnt["se"])), ch="x")
            wld = (("dm_x", ld),)
            c_sq = se(lambda e: e.activation(sq, xt[:], AF.Square, bias=0.0, scale=1.0,
                                             accum_out=s_ssq), waits=wld)
            c_sm = ve(lambda e: e.tensor_reduce(s_sum, xt[:], AX.X, ALU.add), waits=wld)
            ve(lambda e: e.tensor_scalar_mul(s_mean, s_sum, 1.0 / D))
            ve(lambda e: e.tensor_scalar_mul(s_var, s_ssq, 1.0 / D), waits=(("se", c_sq),))
            ve(lambda e: e.tensor_mul(s_m2, s_mean, s_mean))
            ve(lambda e: e.tensor_sub(s_var, s_var, s_m2))
            c_st = se(lambda e: e.activation(s_std, s_var, AF.Sqrt, bias=cEps[:, 0:1], scale=1.0),
                      waits=(("ve", em.cnt["ve"]),))
            c_r = ve(lambda e, tt=tt: e.reciprocal(r_st[:, tt:tt + 1], s_std), waits=(("se", c_st),))
            c_mr = ve(lambda e, tt=tt: e.scalar_tensor_tensor(
                mr_st[:, tt:tt + 1], s_mean, -1.0, r_st[:, tt:tt + 1], ALU.mult, ALU.mult))
            c_xn = ve(lambda e, tt=tt: e.tensor_scalar(
                xn, xt[:], r_st[:, tt:tt + 1], mr_st[:, tt:tt + 1], ALU.mult, ALU.add),
                waits=(("te", em.cnt["te"]),))
            c_am = ve(lambda e: e.tensor_scalar(sq, xn, 1.0, amax, ALU.mult, ALU.max,
                                                accum_out=amax), waits=(("ve", c_xn),))
            if do_xnT:
                for half in range(2):
                    bk = 6 + half
                    tcs = []
                    for j in range(4):
                        cch = half * 4 + j
                        tcs.append(te(lambda e, cch=cch, bk=bk, j=j: e.transpose(
                            ps[:, bk * 512 + j * 128: bk * 512 + j * 128 + 128],
                            xn[:, cch * 128:(cch + 1) * 128], cI[:]),
                            waits=use_bank(bk, (("ve", c_am),))))
                    cev = ve(lambda e, half=half, tt=tt, bk=bk: e.tensor_copy(
                        A[:].rearrange("p (c t) -> p c t", t=2048)[:, half * 4:half * 4 + 4,
                                                                   tt * 128:tt * 128 + 128],
                        ps[:, bk * 512:bk * 512 + 512].rearrange("p (c q) -> p c q", q=128)),
                        waits=(("te", tcs[-1]), ("dm_s", em.cnt["dm_s"])))
                    read_bank(bk, ("ve", cev))
        if do_xnT:
            for cch in range(NDC):
                em.dma("sync", dxnT[cch * 128:(cch + 1) * 128, :],
                       A[:, cch * 2048:(cch + 1) * 2048], waits=(("ve", em.cnt["ve"]),),
                       ch="s")
        ct = te(lambda e: e.transpose(ps[0:1, 3584:3584 + 128], amax, cI[:]),
                waits=use_bank(7, (("ve", em.cnt["ve"]),)))
        cl = ve(lambda e: e.tensor_reduce(smrow[:, 0:1], ps[0:1, 3584:3584 + 128],
                                          AX.X, ALU.max), waits=(("te", ct),))
        read_bank(7, ("ve", cl))
        return cl

    def exchange(n, c_ready, d_in, d_out, gmax_col):
        em.do("gpsimd", lambda e, d_in=d_in: e.dma_start(out=d_in, in_=smrow[0:1, 0:1]),
              waits=(("ve", c_ready),), sem="dmg", inc=16)
        em.do("gpsimd", lambda e, d_in=d_in, d_out=d_out: e.collective_compute(
            "AllReduce", ALU.max, replica_groups=[list(range(8))],
            ins=[d_in.opt()], outs=[d_out.opt()]),
            waits=(("dmg", em.cnt["dmg"]),), sem="cc", inc=1)
        em.do("gpsimd", lambda e, d_out=d_out: e.dma_start(out=smrow[0:1, 1:2], in_=d_out),
              waits=(("cc", em.cnt["cc"]),), sem="dmg", inc=16)
        cg = em.cnt["dmg"]
        ct = te(lambda e, gmax_col=gmax_col: e.matmul(
            ps[:, 3072 + gmax_col:3072 + gmax_col + 1], cOnes[:], smrow[0:1, 1:2],
            start=True, stop=True), waits=use_bank(6, (("dmg", cg),)))
        cv = ve(lambda e, gmax_col=gmax_col: e.tensor_copy(
            sm[:, gmax_col:gmax_col + 1], ps[:, 3072 + gmax_col:3072 + gmax_col + 1]),
            waits=(("te", ct),))
        read_bank(6, ("ve", cv))
        return cv

    def phase_W2():
        raw = w2scr[:, 0:1536]
        hi = w2scr[:, 1536:2304].bitcast(BF16)
        lo = w2scr[:, 2304:3072].bitcast(BF16)
        for nm in ("w1", "w2", "w3"):
            wd = din[f"L1_ffn_{nm}"]
            rows, cols = (HID, D) if nm == "w2" else (D, HID)
            nrc = rows // 128
            nhalf = cols // 1536 if cols % 1536 == 0 else None
            ncc = (cols + 1535) // 1536
            for e_ in range(E_MOE):
                for rc in range(nrc):
                    for cc2 in range(ncc):
                        w0 = cc2 * 1536
                        w1_ = min(1536, cols - w0)
                        ld = em.dma("gpsimd", raw[:, 0:w1_],
                                    wd[e_][rc * 128:(rc + 1) * 128, w0:w0 + w1_],
                                    waits=(("ge", em.cnt["ge"]),), ch="g1", force_ch=True)
                        c1 = em.do("gpsimd", lambda e, w1_=w1_: e.tensor_copy(
                            hi[:, 0:w1_], raw[:, 0:w1_]),
                            waits=(("dm_g1", ld), ("dm_g2", em.cnt["dm_g2"])),
                            sem="ge", inc=1)
                        c2 = em.do("gpsimd", lambda e, w1_=w1_: e.tensor_tensor(
                            lo[:, 0:w1_], raw[:, 0:w1_], hi[:, 0:w1_],
                            ALU.subtract), sem="ge", inc=1)
                        em.dma("gpsimd", dsc[f"d_{nm}hi"][e_][rc * 128:(rc + 1) * 128,
                                                              w0:w0 + w1_],
                               hi[:, 0:w1_], waits=(("ge", c1),), ch="g2", force_ch=True)
                        em.dma("gpsimd", dsc[f"d_{nm}lo"][e_][rc * 128:(rc + 1) * 128,
                                                              w0:w0 + w1_],
                               lo[:, 0:w1_], waits=(("ge", c2),), ch="g2", force_ch=True)
        em.do("gpsimd", lambda e: e.nop(),
              waits=(("dm_g2", em.cnt["dm_g2"]), ("dm_g1", em.cnt["dm_g1"])),
              sem="xd", inc=1)
        em.w2_done = em.cnt["xd"]

    if phase_on():
        phase_W1()
    if phase_on():
        phase_L0A(din["L0_attn_x"], st_r0, st_mr0, dsc["d_xnT"], smax_col=0)
        em.barrier()
    if phase_on():
        c_ready = em.cnt["ve"]
        cg1 = exchange(1, c_ready, dsc["d_cc1i"], dsc["d_cc1o"], gmax_col=2)
        cld = em.dma("sync", G[0:1, 8:9], din["L0_ffn_clip_x"], waits=())
        ct = te(lambda e: e.matmul(ps[:, 3200:3201], cOnes[:], G[0:1, 8:9],
                                   start=True, stop=True),
                waits=use_bank(6, (("dm_c", cld),)))
        cv = ve(lambda e: e.tensor_copy(G[:, 9:10], ps[:, 3200:3201]), waits=(("te", ct),))
        read_bank(6, ("ve", cv))
        cs = se(lambda e: e.activation(G[:, 10:11], G[:, 9:10], AF.Sigmoid, bias=0.0, scale=1.0),
                waits=(("ve", cv),))
        c1 = ve(lambda e: e.tensor_mul(sm[:, 3:4], G[:, 10:11], sm[:, 2:3]),
                waits=(("se", cs), ("ve", cg1)))
        c2 = ve(lambda e: e.tensor_scalar_mul(sm[:, 3:4], sm[:, 3:4], 1.0 / 127.0))
        c3 = ve(lambda e: e.reciprocal(sm[:, 4:5], sm[:, 3:4]))
        em.barrier()

    def phase_L0C():
        xt = Bb[:, 0:1024]
        xn = Bb[:, 1024:2048]
        scr1 = Bb[:, 2048:3072]; scr2 = Bb[:, 3072:4096]
        scr3 = Dd[:, 1024:2048]
        scr_i32 = Dd[:, 0:1024].bitcast(I32)
        xint = C[:, 0:2048].bitcast(BF16)[:, 0:1024]
        for tt in range(NTT):
            ld = em.dma("sync", xt[:], din["L0_attn_x"][tt * 128:(tt + 1) * 128, :],
                        waits=(("ve", em.cnt["ve"]), ("se", em.cnt["se"])), ch="x")
            c_xn = ve(lambda e, tt=tt: e.tensor_scalar(
                xn, xt[:], st_r0[:, tt:tt + 1], st_mr0[:, tt:tt + 1], ALU.mult, ALU.add),
                waits=(("dm_x", ld),))
            cq = quant_chain(xint, xn, sm[:, 4:5], scr1, scr2, scr3, scr_i32,
                             w_in=(("te", em.cnt["te"]),))
            Abf = A[:].bitcast(BF16)
            for half in range(2):
                bk = 6 + half
                tcs = []
                for j in range(4):
                    cch = half * 4 + j
                    tcs.append(te(lambda e, cch=cch, bk=bk, j=j: e.transpose(
                        ps[:, bk * 512:bk * 512 + 256].bitcast(BF16)[:, j * 128:j * 128 + 128],
                        xint[:, cch * 128:(cch + 1) * 128], cIb[:]),
                        waits=use_bank(bk, (("ve", cq),))))
                cev = ve(lambda e, half=half, tt=tt, bk=bk: e.tensor_copy(
                    Abf[:].rearrange("p (c t) -> p c t", t=2048)[:, half * 4:half * 4 + 4,
                                                                tt * 128:tt * 128 + 128],
                    ps[:, bk * 512:bk * 512 + 256].bitcast(BF16).rearrange(
                        "p (c q) -> p c q", q=128)),
                    waits=(("te", tcs[-1]),))
                read_bank(bk, ("ve", cev))
        Abf = A[:].bitcast(BF16)
        for cch in range(NDC):
            em.dma("sync", dsc["d_XintT"][cch * 128:(cch + 1) * 128, :],
                   Abf[:, cch * 2048:(cch + 1) * 2048], waits=(("ve", em.cnt["ve"]),),
                   ch="s")
        em.barrier()

    if phase_on():
        phase_L0C()



    qpair = A[:, 0:2048]
    qlT = {0: A[:, 2048:4096], 1: A[:, 4096:6144]}
    S_sb = A[:, 6144:8192]
    P32g = [A[:, 8192 + j * 2048:8192 + (j + 1) * 2048] for j in range(4)]
    sTh = {0: C[0:64, 0:2048], 1: C[0:64, 2048:4096]}
    srow = C[64:65, 0:2048]
    uT = C[0:64, 4096:6144]
    utok = Dd[:, 0:1040]
    utokv = utok.rearrange("p (k c) -> p k c", c=65)
    rowbuf = Dd[0:1, 1056:3104]
    PT_sb = Ee[:, 0:512]
    mcol = G[:, 16:17]; negm = G[:, 17:18]; sumcol = G[:, 18:19]; rcol = G[:, 19:20]

    def attn_core(cfg):
        causal = cfg["causal"]
        for pair in range(8):
            ldq = em.dma("sync", qpair, cfg["d_q"][pair * 128:(pair + 1) * 128, :],
                         waits=(("te", em.cnt["te"]), ("ve", em.cnt["ve"]),
                                ("dm_s", em.cnt["dm_s"])), ch="t")
            cfg["prep_pair"](pair, ldq)
            for hh in range(2):
                h = pair * 2 + hh
                cfg["prep_head"](pair, hh)
                ve_prep = em.cnt["ve"]
                dmw_prep = em.cnt["dm_w"]
                dmt_prep = em.cnt["dm_t"]
                cex_prev = em.cnt["se"]
                tr_prev = em.cnt["te"]
                def emit_S(qt):
                    rng = (qt + 1) * 128 if causal else T
                    nkc = (rng + 511) // 512
                    cms = []
                    for kc in range(nkc):
                        w = min(512, rng - kc * 512)
                        isdiag = causal and (qt // 4 == kc)
                        m1 = te(lambda e, kc=kc, w=w, qt=qt, hh=hh, pair=pair,
                                isdiag=isdiag: e.matmul(
                            ps[:, kc * 512:kc * 512 + w],
                            cfg["score_lhsT"](pair, hh, qt),
                            cfg["score_rhs"](pair, hh, kc, w),
                            start=True, stop=not isdiag,
                            tile_position=cfg["tile_pos"](hh)),
                            waits=use_bank(kc, (("ve", ve_prep),
                                                ("dm_w", dmw_prep),
                                                ("dm_t", dmt_prep))))
                        if isdiag:
                            m1 = te(lambda e, qt=qt: e.matmul(
                                ps[:, qt * 128:qt * 128 + 128], cI[:], cMd32[:],
                                start=False, stop=True))
                        cms.append(m1)
                    return cms

                cms_next = emit_S(0)
                cexs = [None] * 4
                for qg in range(4):
                    for jq in range(4):
                        qt = qg * 4 + jq
                        rng = (qt + 1) * 128 if causal else T
                        nkc = (rng + 511) // 512
                        cms = cms_next
                        cz = ve(lambda e: e.memset(mcol, -1e30))
                        for kc in range(nkc):
                            w = min(512, rng - kc * 512)
                            cev = ve(lambda e, kc=kc, w=w, sc=cfg["scale"]: e.tensor_scalar(
                                S_sb[:, kc * 512:kc * 512 + w], ps[:, kc * 512:kc * 512 + w],
                                sc, mcol, ALU.mult, ALU.max, accum_out=mcol),
                                waits=(("te", cms[kc]), ("se", cex_prev)))
                            read_bank(kc, ("ve", cev))
                        cn = ve(lambda e: e.tensor_scalar_mul(negm, mcol, -1.0))
                        cex = se(lambda e, rng=rng, jq=jq: e.activation(
                            P32g[jq][:, 0:rng], S_sb[:, 0:rng], AF.Exp, bias=negm,
                            scale=1.0),
                            waits=(("ve", em.cnt["ve"]), ("te", tr_prev)))
                        cex_prev = cex
                        cexs[jq] = cex
                        if qt + 1 < NTT:
                            cms_next = emit_S(qt + 1)
                    nkt = (qg * 4 + 4) if causal else NTT
                    for kt in range(nkt):
                        j0 = max(0, kt - qg * 4) if causal else 0
                        tcs = []
                        for j in range(j0, 4):
                            tcs.append(te(lambda e, kt=kt, j=j: e.transpose(
                                ps[:, 3072 + j * 128:3072 + j * 128 + 128],
                                P32g[j][:, kt * 128:(kt + 1) * 128], cI[:]),
                                waits=use_bank(6, (("se", cexs[j]),))))
                        tr_prev = tcs[-1]
                        cpt = se(lambda e, j0=j0: e.activation(
                            PT_sb[:, j0 * 128:512], ps[:, 3072 + j0 * 128:3584],
                            AF.Copy, bias=0.0, scale=1.0),
                            waits=(("te", tcs[-1]), ("ve", em.cnt["ve"])))
                        read_bank(6, ("se", cpt))
                        my = te(lambda e, kt=kt, j0=j0, nkt=nkt: e.matmul(
                            ps[0:65, 3584 + j0 * 128:4096],
                            utok[:, kt * 65:kt * 65 + 65],
                            PT_sb[:, j0 * 128:512],
                            start=(kt == 0), stop=(kt == nkt - 1),
                            skip_group_check=True),
                            waits=use_bank(7, (("se", cpt),)))
                    cey = ve(lambda e, qg=qg, hh=hh: e.tensor_copy(
                        sTh[hh][:, qg * 512:(qg + 1) * 512],
                        ps[0:64, 3584:4096]), waits=(("te", em.cnt["te"]),))
                    ces = ve(lambda e, qg=qg: e.tensor_copy(
                        srow[:, qg * 512:(qg + 1) * 512],
                        ps[64:65, 3584:4096]))
                    read_bank(7, ("ve", ces))
                crcp = ve(lambda e: e.reciprocal(srow[:], srow[:]))
                cmv = em.dma("sync", rowbuf, srow[:], waits=(("ve", crcp),), ch="t")
                for g in range(4):
                    mo = te(lambda e, g=g: e.matmul(
                        ps[0:64, 3072:3584], cOnes[0:1, 0:64],
                        rowbuf[0:1, g * 512:(g + 1) * 512], start=True, stop=True),
                        waits=use_bank(6, (("dm_t", cmv),)))
                    cno = ve(lambda e, g=g, hh=hh: e.scalar_tensor_tensor(
                        sTh[hh][:, g * 512:(g + 1) * 512],
                        sTh[hh][:, g * 512:(g + 1) * 512], 1.0,
                        ps[0:64, 3072:3584], ALU.mult, ALU.mult),
                        waits=(("te", mo),))
                    read_bank(6, ("ve", cno))
                em.dma("sync", cfg["d_out"][pair * 128 + hh * 64:pair * 128 + hh * 64 + 64, :],
                       sTh[hh][:], waits=(("ve", em.cnt["ve"]),), ch="s")



    def phase_L0P():
        keff = l0w[:, 0:1024]; vmat = l0w[:, 1024:2048]
        WdqT = l0w[:, 2048:3072]; WdkvT = l0w[:, 3072:4096]
        WuqT = Bb[:, 5120:6144]
        wuq = Bb[:, 0:1024]; wuk = Bb[:, 1024:2048]
        wraw = Bb[:, 2048:3072]
        wuv = Bb[:, 3072:4096]; wocc = Bb[:, 4096:5120]
        WoT = C[:]
        csb = Dd[:, 0:128]
        ld1 = em.dma("sync", wuq.rearrange("p (c j) -> p c j", j=128),
                     din["L0_attn_W_uq"].rearrange("(c p) j -> p c j", p=128),
                     waits=(("te", em.cnt["te"]), ("ve", em.cnt["ve"])), ch="w")
        ld2 = em.dma("sync", wuk.rearrange("p (c j) -> p c j", j=128),
                     din["L0_attn_W_uk"].rearrange("(c p) j -> p c j", p=128), ch="w")
        ld3 = em.dma("sync", wraw, din["L0_attn_W_dq"], ch="w")
        ld4 = em.dma("sync", wuv.rearrange("p (c j) -> p c j", j=128),
                     din["L0_attn_W_uv"].rearrange("(c p) j -> p c j", p=128), ch="w")
        mm = None
        for kc in range(8):
            mm = te(lambda e, kc=kc: e.matmul(ps[:, 3072:3200],
                                              wuq[:, kc * 128:(kc + 1) * 128],
                                              wuk[:, kc * 128:(kc + 1) * 128],
                                              start=(kc == 0), stop=(kc == 7)),
                    waits=use_bank(6, (("dm_w", ld2),)))
        cev = ve(lambda e: e.tensor_copy(csb, ps[:, 3072:3200]), waits=(("te", mm),))
        read_bank(6, ("ve", cev))
        for mc in range(8):
            mk = te(lambda e, mc=mc: e.matmul(ps[:, 3072:3200],
                                              wraw[:, mc * 128:(mc + 1) * 128], csb,
                                              start=True, stop=True),
                    waits=use_bank(6, (("dm_w", ld3), ("ve", cev))))
            cev2 = ve(lambda e, mc=mc: e.tensor_copy(keff[:, mc * 128:(mc + 1) * 128],
                                                     ps[:, 3072:3200]), waits=(("te", mk),))
            read_bank(6, ("ve", cev2))
        for wi, (dst,) in enumerate([(WdqT,), (WdkvT,)]):
            if wi == 1:
                ld3 = em.dma("sync", wraw, din["L0_attn_W_dkv"],
                             waits=(("te", em.cnt["te"]),), ch="w")
            for j in range(8):
                tj = te(lambda e, j=j: e.transpose(ps[:, 3584 + (j % 4) * 128:3584 + (j % 4) * 128 + 128],
                                                   wraw[:, j * 128:(j + 1) * 128], cI[:]),
                        waits=use_bank(7, (("dm_w", ld3),)))
                cj = ve(lambda e, j=j, dst=dst: e.tensor_copy(
                    dst[:, j * 128:(j + 1) * 128],
                    ps[:, 3584 + (j % 4) * 128:3584 + (j % 4) * 128 + 128]),
                    waits=(("te", tj),))
                read_bank(7, ("ve", cj))
        for j in range(8):
            tj = te(lambda e, j=j: e.transpose(ps[:, 3584 + (j % 4) * 128:3584 + (j % 4) * 128 + 128],
                                               wuq[:, j * 128:(j + 1) * 128], cI[:]),
                    waits=use_bank(7, (("dm_w", ld1),)))
            cj = ve(lambda e, j=j: e.tensor_copy(
                WuqT[:, j * 128:(j + 1) * 128],
                ps[:, 3584 + (j % 4) * 128:3584 + (j % 4) * 128 + 128]),
                waits=(("te", tj),))
            read_bank(7, ("ve", cj))
        for cc in range(8):
            ldo = em.dma("sync", wocc, din["L0_attn_W_o"][cc * 128:(cc + 1) * 128, :],
                         waits=(("te", em.cnt["te"]),), ch="w")
            for jc in range(8):
                tj = te(lambda e, jc=jc: e.transpose(
                    ps[:, 3584 + (jc % 4) * 128:3584 + (jc % 4) * 128 + 128],
                    wocc[:, jc * 128:(jc + 1) * 128], cI[:]),
                    waits=use_bank(7, (("dm_w", ldo),)))
                cj = ve(lambda e, jc=jc, cc=cc: e.tensor_copy(
                    WoT[:, jc * 1024 + cc * 128: jc * 1024 + cc * 128 + 128],
                    ps[:, 3584 + (jc % 4) * 128:3584 + (jc % 4) * 128 + 128]),
                    waits=(("te", tj),))
                read_bank(7, ("ve", cj))
        for n2 in range(2):
            mm = None
            for jc in range(8):
                mm = te(lambda e, jc=jc, n2=n2: e.matmul(
                    ps[:, 3072:3584], wuv[:, jc * 128:(jc + 1) * 128],
                    WoT[:, jc * 1024 + n2 * 512: jc * 1024 + n2 * 512 + 512],
                    start=(jc == 0), stop=(jc == 7)),
                    waits=use_bank(6, (("ve", em.cnt["ve"]), ("dm_w", ld4))))
            cev = ve(lambda e, n2=n2: e.tensor_copy(vmat[:, n2 * 512:(n2 + 1) * 512],
                                                    ps[:, 3072:3584]), waits=(("te", mm),))
            read_bank(6, ("ve", cev))
        em.barrier()

    def phase_L0F():
        keff = l0w[:, 0:1024]; vmat = l0w[:, 1024:2048]
        WdqT = l0w[:, 2048:3072]; WdkvT = l0w[:, 3072:4096]
        WuqT = Bb[:, 5120:6144]
        xnT = A[:]
        q1T = Ee[:]
        ckvT = Ff[:]
        q2scr = Dd[:, 0:512]
        ld_xn = em.dma("sync", xnT.rearrange("p (c t) -> p c t", t=T),
                       dsc["d_xnT"].rearrange("(c p) t -> p c t", p=128),
                       waits=(("te", em.cnt["te"]), ("ve", em.cnt["ve"]),
                              ("se", em.cnt["se"])), ch="x")
        for dst, lhs in ((q1T, WdqT), (ckvT, WdkvT)):
            for tc in range(4):
                mm = None
                for kc in range(8):
                    mm = te(lambda e, kc=kc, tc=tc, lhs=lhs: e.matmul(
                        ps[:, 0:512], lhs[:, kc * 128:(kc + 1) * 128],
                        xnT[:, kc * 2048 + tc * 512: kc * 2048 + tc * 512 + 512],
                        start=(kc == 0), stop=(kc == 7)),
                        waits=use_bank(0, (("dm_x", ld_xn),)) if kc == 0 else ())
                cv = ve(lambda e, tc=tc, dst=dst: e.tensor_copy(
                    dst[:, tc * 512:(tc + 1) * 512], ps[:, 0:512]), waits=(("te", mm),))
                read_bank(0, ("ve", cv))
        for mc in range(8):
            for tc in range(4):
                mm = te(lambda e, mc=mc, tc=tc: e.matmul(
                    ps[:, 0:512], WuqT[:, mc * 128:(mc + 1) * 128],
                    q1T[:, tc * 512:(tc + 1) * 512], start=True, stop=True),
                    waits=use_bank(0, (("ve", em.cnt["ve"]),)))
                cv = ve(lambda e: e.tensor_copy(q2scr, ps[:, 0:512]),
                        waits=(("te", mm), ("dm_s", em.cnt["dm_s"])))
                read_bank(0, ("ve", cv))
                em.dma("sync", dsc["d_q2T"][mc * 128:(mc + 1) * 128,
                                            tc * 512:(tc + 1) * 512], q2scr,
                       waits=(("ve", cv),), ch="s")
        def prep_pair(pair, ldq):
            for hh in range(2):
                base = hh * 64
                bk = 4 + hh
                for tc in range(4):
                    mm = te(lambda e, tc=tc, base=base, pair=pair, bk=bk: e.matmul(
                        ps[:, bk * 512:bk * 512 + 512],
                        keff[base:base + 64, pair * 128:(pair + 1) * 128],
                        qpair[base:base + 64, tc * 512:(tc + 1) * 512],
                        start=True, stop=True, tile_position=(base, 0)),
                        waits=use_bank(bk, (("dm_t", ldq), ("ve", em.cnt["ve"]))))
                    cv = ve(lambda e, tc=tc, hh=hh, bk=bk: e.tensor_copy(
                        qlT[hh][:, tc * 512:(tc + 1) * 512], ps[:, bk * 512:bk * 512 + 512]),
                        waits=(("te", mm),))
                    read_bank(bk, ("ve", cv))

        def prep_head(pair, hh):
            h = pair * 2 + hh
            for tc in range(4):
                mu = te(lambda e, tc=tc, h=h: e.matmul(
                    ps[0:64, 3072:3584], vmat[:, h * 64:(h + 1) * 64],
                    ckvT[:, tc * 512:(tc + 1) * 512], start=True, stop=True),
                    waits=use_bank(6, (("ve", em.cnt["ve"]),)))
                cuv = ve(lambda e, tc=tc: e.tensor_copy(
                    uT[:, tc * 512:(tc + 1) * 512], ps[0:64, 3072:3584]),
                    waits=(("te", mu),))
                read_bank(6, ("ve", cuv))
            for kt in range(NTT):
                tu = te(lambda e, kt=kt: e.transpose(
                    ps[0:128, 3072:3072 + 64], uT[:, kt * 128:(kt + 1) * 128],
                    cI[0:64, 0:64]),
                    waits=use_bank(6, (("ve", em.cnt["ve"]),)))
                cut = ve(lambda e, kt=kt: e.tensor_copy(
                    utok[:, kt * 65:kt * 65 + 64], ps[:, 3072:3072 + 64]),
                    waits=(("te", tu),))
                read_bank(6, ("ve", cut))
            ve(lambda e: e.memset(utokv[:, :, 64:65], 1.0))

        cfg = {
            "d_q": dsc["d_q2T"], "d_out": dsc["d_sT"], "causal": True,
            "scale": 0.125, "prep_pair": prep_pair, "prep_head": prep_head,
            "score_lhsT": lambda pair, hh, qt: qlT[hh][:, qt * 128:(qt + 1) * 128],
            "score_rhs": lambda pair, hh, kc, w: ckvT[:, kc * 512:kc * 512 + w],
            "tile_pos": lambda hh: None,
        }
        attn_core(cfg)
        em.barrier()

    def phase_L0D():
        Cbf = C[:].bitcast(BF16)
        ld_x = em.dma("sync", Cbf.rearrange("p (c t) -> p c t", t=T),
                      dsc["d_XintT"].rearrange("(c p) t -> p c t", p=128),
                      waits=(("ve", em.cnt["ve"]), ("te", em.cnt["te"])), ch="x")
        Dbf = Dd[:].bitcast(BF16)
        gate_sb = Ee[:, 0:512]
        scr_m = Ee[:, 512:1024]
        h_sb = Ff[:]
        g_sc = G[:, 12:13]; u_sc = G[:, 13:14]
        amax2 = sm[:, 5:6]; amax2n = sm[:, 6:7]
        ve(lambda e: e.memset(amax2, 0.0))
        ve(lambda e: e.memset(amax2n, 0.0))
        ldw_prev = ld_x
        for ic in range(NIC):
            par = ic % 2
            wg = Dbf[:, par * 2048:par * 2048 + 1024]
            wu = Dbf[:, par * 2048 + 1024:par * 2048 + 2048]
            ldg = em.dma("sync", wg.rearrange("p (c j) -> p c j", j=128),
                         dsc["d_WgT"][:, ic * 128:(ic + 1) * 128]
                         .rearrange("(c p) j -> p c j", p=128),
                         waits=(("te", em.cnt["te"]),), ch="w")
            ldu = em.dma("sync", wu.rearrange("p (c j) -> p c j", j=128),
                         dsc["d_WuT"][:, ic * 128:(ic + 1) * 128]
                         .rearrange("(c p) j -> p c j", p=128), ch="w")
            cg = ve(lambda e, ic=ic: e.scalar_tensor_tensor(
                g_sc, sgv[:, ic:ic + 1], 1.0, sm[:, 3:4], ALU.mult, ALU.mult))
            cu = ve(lambda e, ic=ic: e.scalar_tensor_tensor(
                u_sc, suv[:, ic:ic + 1], 1.0, sm[:, 3:4], ALU.mult, ALU.mult))
            for tc in range(4):
                mg = mu = None
                for kc in range(8):
                    mg = te(lambda e, kc=kc, tc=tc, wg=wg: e.matmul(
                        ps[:, 0:512], wg[:, kc * 128:(kc + 1) * 128],
                        Cbf[:, kc * 2048 + tc * 512: kc * 2048 + tc * 512 + 512],
                        start=(kc == 0), stop=(kc == 7)),
                        waits=use_bank(0, (("dm_w", ldu), ("dm_x", ld_x))) if kc == 0 else ())
                for kc in range(8):
                    mu = te(lambda e, kc=kc, tc=tc, wu=wu: e.matmul(
                        ps[:, 512:1024], wu[:, kc * 128:(kc + 1) * 128],
                        Cbf[:, kc * 2048 + tc * 512: kc * 2048 + tc * 512 + 512],
                        start=(kc == 0), stop=(kc == 7)),
                        waits=use_bank(1, (("dm_w", ldu),)) if kc == 0 else ())
                c_si = se(lambda e: e.activation(gate_sb, ps[:, 0:512], AF.Silu,
                                                 bias=0.0, scale=g_sc),
                          waits=(("te", mg), ("ve", em.cnt["ve"])))
                read_bank(0, ("se", c_si))
                c_h = ve(lambda e, tc=tc: e.scalar_tensor_tensor(
                    h_sb[:, tc * 512:(tc + 1) * 512], ps[:, 512:1024], u_sc,
                    gate_sb, ALU.mult, ALU.mult),
                    waits=(("te", mu), ("se", c_si), ("dm_s", em.cnt["dm_s"])))
                read_bank(1, ("ve", c_h))
                c_am = ve(lambda e, tc=tc: e.tensor_scalar(
                    scr_m, h_sb[:, tc * 512:(tc + 1) * 512], 1.0, amax2,
                    ALU.mult, ALU.max, accum_out=amax2))
                c_am2 = ve(lambda e, tc=tc: e.tensor_scalar(
                    scr_m, h_sb[:, tc * 512:(tc + 1) * 512], -1.0, amax2n,
                    ALU.mult, ALU.max, accum_out=amax2n))
            em.dma("sync", dsc["d_h"][ic * 128:(ic + 1) * 128, :], h_sb[:],
                   waits=(("ve", em.cnt["ve"]),), ch="s")
        ve(lambda e: e.tensor_max(amax2, amax2, amax2n))
        ct = te(lambda e: e.transpose(ps[0:1, 3584:3584 + 128], amax2, cI[:]),
                waits=use_bank(7, (("ve", em.cnt["ve"]),)))
        cl = ve(lambda e: e.tensor_reduce(smrow[:, 0:1], ps[0:1, 3584:3584 + 128],
                                          AX.X, ALU.max), waits=(("te", ct),))
        read_bank(7, ("ve", cl))
        em.barrier()
        return cl

    if phase_on():
        phase_L0P()
    if phase_on():
        cl2 = phase_L0D()
    if phase_on():
        cg2 = exchange(2, cl2, dsc["d_cc2i"], dsc["d_cc2o"], gmax_col=7)
        cldc = em.dma("sync", G[0:1, 8:9], din["L0_ffn_clip_down"], waits=())
        ct = te(lambda e: e.matmul(ps[:, 3200:3201], cOnes[:], G[0:1, 8:9],
                                   start=True, stop=True),
                waits=use_bank(6, (("dm_c", cldc),)))
        cv = ve(lambda e: e.tensor_copy(G[:, 9:10], ps[:, 3200:3201]), waits=(("te", ct),))
        read_bank(6, ("ve", cv))
        cs = se(lambda e: e.activation(G[:, 10:11], G[:, 9:10], AF.Sigmoid, bias=0.0, scale=1.0),
                waits=(("ve", cv),))
        ve(lambda e: e.tensor_mul(sm[:, 8:9], G[:, 10:11], sm[:, 7:8]),
           waits=(("se", cs), ("ve", cg2)))
        ve(lambda e: e.tensor_scalar_mul(sm[:, 8:9], sm[:, 8:9], 1.0 / 127.0))
        ve(lambda e: e.reciprocal(sm[:, 9:10], sm[:, 8:9]))
        em.barrier()
        phase_W2()
    if phase_on() and os.environ.get("KD_FAKE_ST", "0") != "1":
        phase_L0F()

    def phase_L0G():
        for hc in range(NDC):
            ve(lambda e, hc=hc: e.scalar_tensor_tensor(
                sdv[:, hc:hc + 1], sdv[:, hc:hc + 1], 1.0, sm[:, 8:9], ALU.mult, ALU.mult))
        ht = A[:, 0:2816]
        htv = A[:, 0:11264].rearrange("p (c f) -> p c f", f=512)
        scr1 = A[:, 11264:11776]; scr2 = A[:, 11776:12288]
        scr3 = A[:, 12288:12800]
        scr_i32 = A[:, 12800:13312].bitcast(I32)
        hint = Bb[:, 0:5632].bitcast(BF16)
        hintv = hint.rearrange("p (c f) -> p c f", f=512)
        wdt = Dd[:].bitcast(BF16)
        gT = Ee[:, 0:512]
        sTt = Ff[:, 0:512]
        dT = Ee[:, 512:1024]
        xtile = C[:, 0:1024]
        dtok = C[:, 1024:5120]
        for tc in range(4):
            ldh = em.dma("sync", htv, dsc["d_h"][:, tc * 512:(tc + 1) * 512]
                         .rearrange("(c p) f -> p c f", p=128),
                         waits=(("te", em.cnt["te"]), ("ve", em.cnt["ve"])), ch="h")
            cq = None
            for icc in range(NIC):
                cq = quant_chain(hintv[:, icc], htv[:, icc], sm[:, 9:10],
                                 scr1, scr2, scr3, scr_i32,
                                 w_in=(("dm_h", ldh),))
            for hc in range(NDC):
                par = hc % 2
                wd = wdt[:, par * 4096:par * 4096 + 2816]
                ldw = em.dma("sync", wd.rearrange("p (c j) -> p c j", j=128),
                             dsc["d_WdT"][:, hc * 128:(hc + 1) * 128]
                             .rearrange("(c p) j -> p c j", p=128),
                             waits=(("te", em.cnt["te"]),), ch="w")
                mm = None
                for icc in range(NIC):
                    mm = te(lambda e, icc=icc, wd=wd: e.matmul(
                        ps[:, 0:512], wd[:, icc * 128:(icc + 1) * 128],
                        hintv[:, icc], start=(icc == 0), stop=(icc == NIC - 1)),
                        waits=use_bank(0, (("dm_w", ldw), ("ve", cq))) if icc == 0 else ())
                c_g = se(lambda e, hc=hc: e.activation(gT, ps[:, 0:512], AF.Silu,
                                                       bias=0.0, scale=sdv[:, hc:hc + 1]),
                         waits=(("te", mm), ("ve", em.cnt["ve"])))
                read_bank(0, ("se", c_g))
                lds = em.dma("sync", sTt, dsc["d_sT"][hc * 128:(hc + 1) * 128,
                                                      tc * 512:(tc + 1) * 512],
                             waits=(("ve", em.cnt["ve"]),), ch="t")
                c_d = ve(lambda e: e.tensor_tensor(dT, gT, sTt, ALU.mult),
                         waits=(("se", c_g), ("dm_t", lds)))
                for j in range(4):
                    tc_t = te(lambda e, j=j: e.transpose(
                        ps[:, 512 + j * 128:512 + (j + 1) * 128],
                        dT[:, j * 128:(j + 1) * 128], cI[:]),
                        waits=use_bank(1, (("ve", c_d),)))
                c_ev = ve(lambda e, hc=hc: e.tensor_copy(
                    dtok.rearrange("p (tt f) -> p tt f", f=1024)[:, :, hc * 128:(hc + 1) * 128],
                    ps[:, 512:1024].rearrange("p (tt f) -> p tt f", f=128)),
                    waits=(("te", em.cnt["te"]),))
                read_bank(1, ("ve", c_ev))
            for j in range(4):
                tt = tc * 4 + j
                ldx = em.dma("sync", xtile, din["L0_attn_x"][tt * 128:(tt + 1) * 128, :],
                             waits=(("ve", em.cnt["ve"]),), ch="x")
                c_x = ve(lambda e, j=j: e.tensor_tensor(
                    C[:, 5120:6144], xtile, dtok[:, j * 1024:(j + 1) * 1024], ALU.add),
                    waits=(("dm_x", ldx), ("dm_s", em.cnt["dm_s"])))
                em.dma("sync", dsc["d_x"][tt * 128:(tt + 1) * 128, :], C[:, 5120:6144],
                       waits=(("ve", c_x),), ch="s")
        em.barrier()



    def emit_oproj(w_dram, dst_dram):
        wbuf = C[:]
        yT = A[:]
        ldw = em.dma("sync", wbuf.rearrange("p (kc m) -> p kc m", m=1024),
                     w_dram.rearrange("(kc p) m -> p kc m", p=128),
                     waits=(("te", em.cnt["te"]), ("ve", em.cnt["ve"])), ch="w")
        ldy = em.dma("sync", yT.rearrange("p (c t) -> p c t", t=T),
                     dsc["d_yT"].rearrange("(c p) t -> p c t", p=128),
                     waits=(("te", em.cnt["te"]), ("ve", em.cnt["ve"]),
                            ("se", em.cnt["se"])), ch="x")
        oscr = Dd[:, 0:512]
        for mc in range(8):
            for tc in range(4):
                mm = None
                for kc in range(8):
                    mm = te(lambda e, kc=kc, mc=mc, tc=tc: e.matmul(
                        ps[:, 0:512], wbuf[:, kc * 1024 + mc * 128: kc * 1024 + mc * 128 + 128],
                        yT[:, kc * 2048 + tc * 512: kc * 2048 + tc * 512 + 512],
                        start=(kc == 0), stop=(kc == 7)),
                        waits=use_bank(0, (("dm_w", ldw), ("dm_x", ldy))) if kc == 0 else ())
                cv = ve(lambda e: e.tensor_copy(oscr, ps[:, 0:512]),
                        waits=(("te", mm), ("dm_s", em.cnt["dm_s"])))
                read_bank(0, ("ve", cv))
                em.dma("sync", dst_dram[mc * 128:(mc + 1) * 128, tc * 512:(tc + 1) * 512],
                       oscr, waits=(("ve", cv),), ch="s")
        em.barrier()

    def phase_L1B():
        xnT = A[:]
        wbuf = C[:]
        ld_xn = em.dma("sync", xnT.rearrange("p (c t) -> p c t", t=T),
                       dsc["d_xnT"].rearrange("(c p) t -> p c t", p=128),
                       waits=(("te", em.cnt["te"]), ("ve", em.cnt["ve"]),
                              ("se", em.cnt["se"])), ch="x")
        ldw = em.dma("sync", wbuf.rearrange("p (kc m) -> p kc m", m=1024),
                     din["L1_attn_Wq"].rearrange("(kc p) m -> p kc m", p=128),
                     waits=(("te", em.cnt["te"]), ("ve", em.cnt["ve"])), ch="w")
        qscr = Dd[:, 0:512]
        for mc in range(8):
            for tc in range(4):
                mm = None
                for kc in range(8):
                    mm = te(lambda e, kc=kc, mc=mc, tc=tc: e.matmul(
                        ps[:, 0:512], wbuf[:, kc * 1024 + mc * 128: kc * 1024 + mc * 128 + 128],
                        xnT[:, kc * 2048 + tc * 512: kc * 2048 + tc * 512 + 512],
                        start=(kc == 0), stop=(kc == 7)),
                        waits=use_bank(0, (("dm_w", ldw), ("dm_x", ld_xn))) if kc == 0 else ())
                cv = ve(lambda e: e.tensor_copy(qscr, ps[:, 0:512]),
                        waits=(("te", mm), ("dm_s", em.cnt["dm_s"])))
                read_bank(0, ("ve", cv))
                em.dma("sync", dsc["d_qT1"][mc * 128:(mc + 1) * 128,
                                            tc * 512:(tc + 1) * 512], qscr,
                       waits=(("ve", cv),), ch="s")
        wkv = Dd[:]
        ldkv = em.dma("sync", wkv.rearrange("p (kc m) -> p kc m", m=512),
                      din["L1_attn_Wkv"].rearrange("(kc p) m -> p kc m", p=128),
                      waits=(("te", em.cnt["te"]), ("ve", em.cnt["ve"]),
                             ("dm_s", em.cnt["dm_s"])), ch="w")
        kscr = Ee[:, 0:512]
        for mc2 in range(2):
            for tc in range(4):
                mm = None
                for kc in range(8):
                    mm = te(lambda e, kc=kc, mc2=mc2, tc=tc: e.matmul(
                        ps[:, 0:512], wkv[:, kc * 512 + mc2 * 128: kc * 512 + mc2 * 128 + 128],
                        xnT[:, kc * 2048 + tc * 512: kc * 2048 + tc * 512 + 512],
                        start=(kc == 0), stop=(kc == 7)),
                        waits=use_bank(0, (("dm_w", ldkv),)))
                cv = ve(lambda e: e.tensor_copy(kscr, ps[:, 0:512]),
                        waits=(("te", mm), ("dm_s", em.cnt["dm_s"])))
                read_bank(0, ("ve", cv))
                em.dma("sync", dsc["d_kT1"][mc2 * 128:(mc2 + 1) * 128,
                                            tc * 512:(tc + 1) * 512], kscr,
                       waits=(("ve", cv),), ch="s")
        vtok = C[:, 4096:8192]
        for tt in range(NTT):
            mm = None
            for kc in range(8):
                mm = te(lambda e, kc=kc, tt=tt: e.matmul(
                    ps[:, 0:256], xnT[:, kc * 2048 + tt * 128: kc * 2048 + tt * 128 + 128],
                    wkv[:, kc * 512 + 256: kc * 512 + 512],
                    start=(kc == 0), stop=(kc == 7)),
                    waits=use_bank(0, (("dm_w", ldkv),)))
            cv = ve(lambda e, tt=tt: e.tensor_copy(vtok[:, tt * 256:(tt + 1) * 256],
                                                   ps[:, 0:256]), waits=(("te", mm),))
            read_bank(0, ("ve", cv))
        em.barrier()
        kdup = Bb[:]
        for kv in range(4):
            for half in range(2):
                em.dma("sync", kdup[half * 64:half * 64 + 64, kv * 2048:(kv + 1) * 2048],
                       dsc["d_kT1"][kv * 64:(kv + 1) * 64, :],
                       waits=(("te", em.cnt["te"]),), ch="w")
        ckd = em.cnt["dm_w"]

        def prep_pair(pair, ldq):
            pass

        def prep_head(pair, hh):
            h = pair * 2 + hh
            kv = h // 4
            for tt in range(NTT):
                ve(lambda e, tt=tt, h=h, kv=kv: e.tensor_scalar_mul(
                    utok[:, tt * 65:tt * 65 + 64],
                    vtok[:, tt * 256 + kv * 64: tt * 256 + kv * 64 + 64],
                    cFtab[:, tt * NH + h:tt * NH + h + 1]),
                    waits=(("te", em.cnt["te"]),))
            ve(lambda e, h=h: e.tensor_copy(
                utokv[:, :, 64:65],
                cFtab[:, h:256:NH].unsqueeze(2)))

        cfg = {
            "d_q": dsc["d_qT1"], "d_out": dsc["d_yT"], "causal": False,
            "scale": 0.125, "prep_pair": prep_pair, "prep_head": prep_head,
            "score_lhsT": lambda pair, hh, qt: qpair[hh * 64:hh * 64 + 64,
                                                     qt * 128:(qt + 1) * 128],
            "score_rhs": lambda pair, hh, kc, w: kdup[
                hh * 64:hh * 64 + 64,
                ((pair * 2 + hh) // 4) * 2048 + kc * 512:
                ((pair * 2 + hh) // 4) * 2048 + kc * 512 + w],
            "tile_pos": lambda hh: (hh * 64, 0),
        }
        attn_core(cfg)
        em.barrier()
        emit_oproj(din["L1_attn_Wo"], dsc["d_sT"])

    def emit_gbcast(g_dram, dst):
        ldg = em.dma("sync", G[0:1, 512:1536] if False else Ff[0:1, 0:1024], g_dram,
                     waits=(("ve", em.cnt["ve"]), ("te", em.cnt["te"])), ch="c")
        for n2 in range(2):
            mo = te(lambda e, n2=n2: e.matmul(
                ps[:, 3072:3584], cOnes[:], Ff[0:1, n2 * 512:(n2 + 1) * 512],
                start=True, stop=True),
                waits=use_bank(6, (("dm_c", ldg),)))
            cv = ve(lambda e, n2=n2: e.tensor_copy(dst[:, n2 * 512:(n2 + 1) * 512],
                                                   ps[:, 3072:3584]), waits=(("te", mo),))
            read_bank(6, ("ve", cv))

    def phase_normT(src_dram, dst_dram, gb, rr_st, ln_stats):
        xt = Bb[:, 0:1024]
        t1 = Bb[:, 1024:2048]
        t2 = Bb[:, 2048:3072]
        sq = Bb[:, 3072:4096]
        s_ssq = G[:, 24:25]; s_ms = G[:, 25:26]; s_rt = G[:, 26:27]
        for tt in range(NTT):
            ld = em.dma("sync", xt, src_dram[tt * 128:(tt + 1) * 128, :],
                        waits=(("ve", em.cnt["ve"]), ("se", em.cnt["se"])), ch="x")
            if ln_stats is not None:
                r_st_, mr_st_ = ln_stats
                cx = ve(lambda e, tt=tt: e.tensor_scalar(
                    t1, xt, r_st_[:, tt:tt + 1], mr_st_[:, tt:tt + 1], ALU.mult, ALU.add),
                    waits=(("dm_x", ld), ("te", em.cnt["te"])))
                vsrc = t1
            else:
                cx = ve(lambda e: e.tensor_copy(t1, xt),
                        waits=(("dm_x", ld), ("te", em.cnt["te"])))
                vsrc = t1
            c_sq = se(lambda e, vsrc=vsrc: e.activation(sq, vsrc, AF.Square, bias=0.0,
                                                        scale=1.0, accum_out=s_ssq),
                      waits=(("ve", cx),))
            ve(lambda e: e.tensor_scalar_mul(s_ms, s_ssq, 1.0 / D), waits=(("se", c_sq),))
            c_st = se(lambda e: e.activation(s_rt, s_ms, AF.Sqrt, bias=cEps[:, 1:2],
                                             scale=1.0), waits=(("ve", em.cnt["ve"]),))
            c_r = ve(lambda e, tt=tt: e.reciprocal(rr_st[:, tt:tt + 1], s_rt),
                     waits=(("se", c_st),))
            c_t2 = ve(lambda e, tt=tt: e.tensor_scalar_mul(t2, t1, rr_st[:, tt:tt + 1]))
            c_v = ve(lambda e: e.tensor_tensor(t1, t2, gb, ALU.mult))
            for half in range(2):
                bk = 6 + half
                tcs = []
                for j in range(4):
                    cch = half * 4 + j
                    tcs.append(te(lambda e, cch=cch, bk=bk, j=j: e.transpose(
                        ps[:, bk * 512 + j * 128: bk * 512 + j * 128 + 128],
                        t1[:, cch * 128:(cch + 1) * 128], cI[:]),
                        waits=use_bank(bk, (("ve", c_v),))))
                cev = ve(lambda e, half=half, tt=tt, bk=bk: e.tensor_copy(
                    A[:].rearrange("p (c t) -> p c t", t=2048)[:, half * 4:half * 4 + 4,
                                                               tt * 128:tt * 128 + 128],
                    ps[:, bk * 512:bk * 512 + 512].rearrange("p (c q) -> p c q", q=128)),
                    waits=(("te", tcs[-1]), ("dm_s", em.cnt["dm_s"])))
                read_bank(bk, ("ve", cev))
        for cch in range(NDC):
            em.dma("sync", dst_dram[cch * 128:(cch + 1) * 128, :],
                   A[:, cch * 2048:(cch + 1) * 2048], waits=(("ve", em.cnt["ve"]),),
                   ch="s")
        em.barrier()

    def phase_L1D():
        rmsT = A[:]
        wbuf = C[:]
        ld_r = em.dma("sync", rmsT.rearrange("p (c t) -> p c t", t=T),
                      dsc["d_rms1T"].rearrange("(c p) t -> p c t", p=128),
                      waits=(("te", em.cnt["te"]), ("ve", em.cnt["ve"]),
                             ("se", em.cnt["se"])), ch="x")
        ldw = em.dma("sync", wbuf.rearrange("p (kc m) -> p kc m", m=1024),
                     din["L1_ffn_wq"].rearrange("(kc p) m -> p kc m", p=128),
                     waits=(("te", em.cnt["te"]), ("ve", em.cnt["ve"])), ch="w")
        qr = Ee[:, 0:512]
        tco = Ff[:, 0:512]; tsi = Ff[:, 512:1024]
        evs = Dd[:, 0:512]
        def rope_block(psrc, tt, dst_sb):
            cosb = cCos[:, tt * 64:(tt + 1) * 64].unsqueeze(1).broadcast_to((128, 8, 64))
            sinb = cSin[:, tt * 64:(tt + 1) * 64].unsqueeze(1).broadcast_to((128, 8, 64))
            pv = psrc.rearrange("p (hd d) -> p hd d", d=64)
            c1 = ve(lambda e: e.tensor_tensor(tco[:].rearrange("p (hd d) -> p hd d", d=64),
                                              pv, cosb, ALU.mult),
                    waits=(("se", em.cnt["se"]), ("te", em.cnt["te"])))
            c2 = ve(lambda e: e.tensor_tensor(
                tsi[:].rearrange("p (hd d) -> p hd d", d=64)[:, :, 0:32],
                pv[:, :, 32:64], sinb[:, :, 0:32], ALU.mult))
            c3 = ve(lambda e: e.tensor_tensor(
                tsi[:].rearrange("p (hd d) -> p hd d", d=64)[:, :, 32:64],
                pv[:, :, 0:32], sinb[:, :, 32:64], ALU.mult))
            c4 = ve(lambda e, dst_sb=dst_sb: e.tensor_tensor(dst_sb, tco[:], tsi[:], ALU.add))
            return c4

        for tt in range(NTT):
            for fc in range(2):
                mm = None
                for kc in range(8):
                    mm = te(lambda e, kc=kc, tt=tt, fc=fc: e.matmul(
                        ps[:, 1024:1536],
                        rmsT[:, kc * 2048 + tt * 128: kc * 2048 + tt * 128 + 128],
                        wbuf[:, kc * 1024 + fc * 512: kc * 1024 + fc * 512 + 512],
                        start=(kc == 0), stop=(kc == 7)),
                        waits=use_bank(2, (("dm_w", ldw), ("dm_x", ld_r))))
                cr4 = rope_block(ps[:, 1024:1536], tt, qr[:])
                read_bank(2, ("ve", cr4))
                tcs = []
                for j in range(4):
                    tcs.append(te(lambda e, j=j: e.transpose(
                        ps[:, 3072 + j * 128:3072 + j * 128 + 128],
                        qr[:, j * 128:(j + 1) * 128], cI[:]),
                        waits=use_bank(6, (("ve", cr4),))))
                cev = ve(lambda e: e.tensor_copy(
                    evs[:].rearrange("p (j t) -> p j t", t=128),
                    ps[:, 3072:3584].rearrange("p (j t) -> p j t", t=128)),
                    waits=(("te", tcs[-1]), ("dm_s", em.cnt["dm_s"])))
                read_bank(6, ("ve", cev))
                em.dma("sync",
                       dsc["d_qrT"][fc * 512:(fc + 1) * 512, tt * 128:(tt + 1) * 128]
                       .rearrange("(j p) t -> p j t", p=128),
                       evs[:].rearrange("p (j t) -> p j t", t=128),
                       waits=(("ve", cev),), ch="s")
        ldwk = em.dma("sync", wbuf[:, 0:2048].rearrange("p (kc m) -> p kc m", m=256),
                      din["L1_ffn_wk"].rearrange("(kc p) m -> p kc m", p=128),
                      waits=(("te", em.cnt["te"]), ("ve", em.cnt["ve"])), ch="w")
        for tt in range(NTT):
            mm = None
            for kc in range(8):
                mm = te(lambda e, kc=kc, tt=tt: e.matmul(
                    ps[:, 1024:1280],
                    rmsT[:, kc * 2048 + tt * 128: kc * 2048 + tt * 128 + 128],
                    wbuf[:, kc * 256:(kc + 1) * 256],
                    start=(kc == 0), stop=(kc == 7)),
                    waits=use_bank(2, (("dm_w", ldwk), ("dm_x", ld_r))))
            cosb = cCos[:, tt * 64:(tt + 1) * 64].unsqueeze(1).broadcast_to((128, 4, 64))
            sinb = cSin[:, tt * 64:(tt + 1) * 64].unsqueeze(1).broadcast_to((128, 4, 64))
            pv = ps[:, 1024:1280].rearrange("p (hd d) -> p hd d", d=64)
            c1 = ve(lambda e, cosb=cosb, pv=pv: e.tensor_tensor(
                tco[:, 0:256].rearrange("p (hd d) -> p hd d", d=64), pv, cosb, ALU.mult),
                waits=(("se", em.cnt["se"]), ("te", em.cnt["te"])))
            c2 = ve(lambda e, sinb=sinb, pv=pv: e.tensor_tensor(
                tsi[:, 0:256].rearrange("p (hd d) -> p hd d", d=64)[:, :, 0:32],
                pv[:, :, 32:64], sinb[:, :, 0:32], ALU.mult))
            c3 = ve(lambda e, sinb=sinb, pv=pv: e.tensor_tensor(
                tsi[:, 0:256].rearrange("p (hd d) -> p hd d", d=64)[:, :, 32:64],
                pv[:, :, 0:32], sinb[:, :, 32:64], ALU.mult))
            c4 = ve(lambda e: e.tensor_tensor(qr[:, 0:256], tco[:, 0:256],
                                              tsi[:, 0:256], ALU.add))
            read_bank(2, ("ve", c4))
            tcs = []
            for j in range(2):
                tcs.append(te(lambda e, j=j: e.transpose(
                    ps[:, 3072 + j * 128:3072 + j * 128 + 128],
                    qr[:, j * 128:(j + 1) * 128], cI[:]),
                    waits=use_bank(6, (("ve", c4),))))
            cev = ve(lambda e: e.tensor_copy(
                evs[:, 0:256].rearrange("p (j t) -> p j t", t=128),
                ps[:, 3072:3328].rearrange("p (j t) -> p j t", t=128)),
                waits=(("te", tcs[-1]), ("dm_s", em.cnt["dm_s"])))
            read_bank(6, ("ve", cev))
            em.dma("sync",
                   dsc["d_krT"][:, tt * 128:(tt + 1) * 128]
                   .rearrange("(j p) t -> p j t", p=128),
                   evs[:, 0:256].rearrange("p (j t) -> p j t", t=128),
                   waits=(("ve", cev),), ch="s")
        ldwv = em.dma("sync", wbuf[:, 0:2048].rearrange("p (kc m) -> p kc m", m=256),
                      din["L1_ffn_wv"].rearrange("(kc p) m -> p kc m", p=128),
                      waits=(("te", em.cnt["te"]), ("ve", em.cnt["ve"])), ch="w")
        vtok = C[:, 4096:8192]
        for tt in range(NTT):
            mm = None
            for kc in range(8):
                mm = te(lambda e, kc=kc, tt=tt: e.matmul(
                    ps[:, 0:256], rmsT[:, kc * 2048 + tt * 128: kc * 2048 + tt * 128 + 128],
                    wbuf[:, kc * 256:(kc + 1) * 256],
                    start=(kc == 0), stop=(kc == 7)),
                    waits=use_bank(0, (("dm_w", ldwv),)))
            cv = ve(lambda e, tt=tt: e.tensor_copy(vtok[:, tt * 256:(tt + 1) * 256],
                                                   ps[:, 0:256]), waits=(("te", mm),))
            read_bank(0, ("ve", cv))
        em.barrier()
        kdup = Bb[:]
        for kv in range(4):
            for half in range(2):
                em.dma("sync", kdup[half * 64:half * 64 + 64, kv * 2048:(kv + 1) * 2048],
                       dsc["d_krT"][kv * 64:(kv + 1) * 64, :],
                       waits=(("te", em.cnt["te"]),), ch="w")

        def prep_pair(pair, ldq):
            pass

        def prep_head(pair, hh):
            h = pair * 2 + hh
            kv = h // 4
            for tt in range(NTT):
                ve(lambda e, tt=tt, kv=kv: e.tensor_copy(
                    utok[:, tt * 65:tt * 65 + 64],
                    vtok[:, tt * 256 + kv * 64: tt * 256 + kv * 64 + 64]),
                    waits=(("te", em.cnt["te"]),))
            ve(lambda e: e.memset(utokv[:, :, 64:65], 1.0))

        cfg = {
            "d_q": dsc["d_qrT"], "d_out": dsc["d_yT"], "causal": True,
            "scale": 0.125, "prep_pair": prep_pair, "prep_head": prep_head,
            "score_lhsT": lambda pair, hh, qt: qpair[hh * 64:hh * 64 + 64,
                                                     qt * 128:(qt + 1) * 128],
            "score_rhs": lambda pair, hh, kc, w: kdup[
                hh * 64:hh * 64 + 64,
                ((pair * 2 + hh) // 4) * 2048 + kc * 512:
                ((pair * 2 + hh) // 4) * 2048 + kc * 512 + w],
            "tile_pos": lambda hh: (hh * 64, 0),
        }
        attn_core(cfg)
        em.barrier()
        emit_oproj(din["L1_ffn_wo"], dsc["d_siT"])

    def phase_L1E():
        b1 = Bb[:, 0:512]; b2 = Bb[:, 512:1024]; b3 = Bb[:, 1024:1536]
        for cc in range(NDC):
            for tc in range(4):
                l1 = em.dma("sync", b1, dsc["d_xnT"][cc * 128:(cc + 1) * 128,
                                                     tc * 512:(tc + 1) * 512],
                            waits=(("ve", em.cnt["ve"]),), ch="x")
                l2 = em.dma("sync", b2, dsc["d_siT"][cc * 128:(cc + 1) * 128,
                                                     tc * 512:(tc + 1) * 512],
                            waits=(("ve", em.cnt["ve"]),), ch="h")
                ca = ve(lambda e: e.tensor_tensor(b3, b1, b2, ALU.add),
                        waits=(("dm_x", l1), ("dm_h", l2), ("dm_s", em.cnt["dm_s"])))
                em.dma("sync", dsc["d_xmidT"][cc * 128:(cc + 1) * 128,
                                              tc * 512:(tc + 1) * 512], b3,
                       waits=(("ve", ca),), ch="s")
        em.barrier()
        xm = Bb[:, 0:1024]
        for tt in range(NTT):
            ldc = em.dma("sync", Dd[:, 0:1024].rearrange("p (c t) -> p c t", t=128),
                         dsc["d_xmidT"][:, tt * 128:(tt + 1) * 128]
                         .rearrange("(c p) t -> p c t", p=128),
                         waits=(("te", em.cnt["te"]), ("ve", em.cnt["ve"])), ch="x")
            for half in range(2):
                bk = 6 + half
                tcs = []
                for j in range(4):
                    cch = half * 4 + j
                    tcs.append(te(lambda e, cch=cch, bk=bk, j=j: e.transpose(
                        ps[:, bk * 512 + j * 128: bk * 512 + j * 128 + 128],
                        Dd[:, cch * 128:(cch + 1) * 128], cI[:]),
                        waits=use_bank(bk, (("dm_x", ldc), ("ve", em.cnt["ve"])))))
                cev = ve(lambda e, half=half, bk=bk: e.tensor_copy(
                    xm[:, half * 512:(half + 1) * 512],
                    ps[:, bk * 512:bk * 512 + 512]),
                    waits=(("te", tcs[-1]), ("dm_s", em.cnt["dm_s"])))
                read_bank(bk, ("ve", cev))
            em.dma("sync", dsc["d_xmid"][tt * 128:(tt + 1) * 128, :], xm,
                   waits=(("ve", em.cnt["ve"]),), ch="s")
        em.barrier()

    def phase_L1F():
        rms2T = A[:]
        ldr = em.dma("sync", rms2T.rearrange("p (c t) -> p c t", t=T),
                     dsc["d_rms2T"].rearrange("(c p) t -> p c t", p=128),
                     waits=(("te", em.cnt["te"]), ("ve", em.cnt["ve"]),
                            ("se", em.cnt["se"])), ch="x")
        gte = Ff[:, 0:32]
        ldg = em.dma("sync", gte.rearrange("p (c j) -> p c j", j=4),
                     din["L1_ffn_gate"].rearrange("(c p) j -> p c j", p=128),
                     waits=(("ve", em.cnt["ve"]),), ch="w")
        logit = Ee[:, 0:4]; prob = Ee[:, 4:8]; eq1 = Ee[:, 8:12]; eq2 = Ee[:, 12:16]
        p2 = Ee[:, 16:20]
        m1 = G[:, 30:31]; m2 = G[:, 31:32]; ssum = G[:, 32:33]
        a1 = G[:, 33:34]; a2 = G[:, 34:35]; rden = G[:, 35:36]
        for tt in range(NTT):
            mm = None
            for kc in range(8):
                mm = te(lambda e, kc=kc, tt=tt: e.matmul(
                    ps[:, 0:4], rms2T[:, kc * 2048 + tt * 128: kc * 2048 + tt * 128 + 128],
                    gte[:, kc * 4:(kc + 1) * 4], start=(kc == 0), stop=(kc == 7)),
                    waits=use_bank(0, (("dm_w", ldg), ("dm_x", ldr))))
            cl_ = ve(lambda e: e.tensor_copy(logit, ps[:, 0:4]), waits=(("te", mm),))
            read_bank(0, ("ve", cl_))
            ve(lambda e: e.tensor_reduce(m1, logit, AX.X, ALU.max))
            ve(lambda e: e.tensor_scalar_mul(m2, m1, -1.0))
            cp = se(lambda e: e.activation(prob, logit, AF.Exp, bias=m2, scale=1.0,
                                           accum_out=ssum), waits=(("ve", em.cnt["ve"]),))
            cv = ve(lambda e: e.reciprocal(rden, ssum), waits=(("se", cp),))
            ve(lambda e: e.tensor_scalar_mul(prob, prob, rden))
            ve(lambda e: e.tensor_reduce(m1, prob, AX.X, ALU.max))
            ve(lambda e: e.tensor_scalar(eq1, prob, m1, None, ALU.is_ge))
            ve(lambda e: e.scalar_tensor_tensor(p2, eq1, -3.0, prob, ALU.mult, ALU.add))
            ve(lambda e: e.tensor_reduce(m2, p2, AX.X, ALU.max))
            ve(lambda e: e.tensor_scalar(eq2, p2, m2, None, ALU.is_ge))
            ve(lambda e: e.tensor_tensor(rden, m1, m2, ALU.add))
            ve(lambda e: e.reciprocal(rden, rden))
            ve(lambda e: e.tensor_tensor(a1, m1, rden, ALU.mult))
            ve(lambda e: e.tensor_tensor(a2, m2, rden, ALU.mult))
            cc1 = ve(lambda e: e.tensor_scalar_mul(eq1, eq1, a1))
            cc2 = ve(lambda e: e.tensor_scalar_mul(eq2, eq2, a2))
            cc3 = ve(lambda e: e.tensor_tensor(eq1, eq1, eq2, ALU.add))
            tc_ = te(lambda e, tt=tt: e.transpose(ps[0:4, 3072 + 0:3072 + 128],
                                                  eq1, cI[:]),
                     waits=use_bank(6, (("ve", cc3),)))
            cev = ve(lambda e, tt=tt: e.tensor_copy(Ee[0:4, 20:148],
                                                    ps[0:4, 3072:3200]),
                     waits=(("te", tc_), ("dm_s", em.cnt["dm_s"])))
            read_bank(6, ("ve", cev))
            em.dma("sync", dsc["d_comb"][:, tt * 128:(tt + 1) * 128], Ee[0:4, 20:148],
                   waits=(("ve", cev),), ch="s")
        em.barrier()
        h_hi = A[:, 0:6144].bitcast(BF16)
        h_lo = A[:, 6144:12288].bitcast(BF16)
        rms2tc = C[:, 0:4096]
        r2hi = C[:, 4096:6144].bitcast(BF16)
        r2lo = C[:, 6144:8192].bitcast(BF16)
        Dbf2 = Dd[:].bitcast(BF16)
        def w13slab(par, mat, part):
            idx = {("w1", "hi"): 0, ("w1", "lo"): 1, ("w3", "hi"): 2, ("w3", "lo"): 3}[(mat, part)]
            return Dbf2[:, par * 4096 + idx * 1024: par * 4096 + (idx + 1) * 1024]
        w2bf = A[:, 12288:15360].bitcast(BF16)
        def w2slab(par, part):
            return w2bf[:, par * 3072 + (0 if part == "hi" else 1536):
                        par * 3072 + (1536 if part == "hi" else 3072)]
        silu_sb = Ee[:, 0:512]; usc = Ee[:, 512:1024]
        h32 = Ee[:, 1536:2048]
        cbc = Ff[:, 0:2048]
        macc = Bb[:, 0:4096]
        xmT = Bb[:, 4096:4608]; xoT = Bb[:, 4608:5120]
        gsil = Bb[:, 5120:5632]; sTt = Bb[:, 5632:6144]; dT = Bb[:, 6144:6656]
        devac = Bb[:, 6656:7168]
        for tt in range(NTT):
            em.dma("sync", d_y[tt * 128:(tt + 1) * 128, :],
                   dsc["d_x"][tt * 128:(tt + 1) * 128, :],
                   waits=(), ch="s")
        xst = em.cnt["dm_s"]
        em.do("sync", lambda e: e.nop(), waits=(("xd", em.w2_done),))
        for tc in range(4):
            ldr2 = em.dma("sync", rms2tc.rearrange("p (kc f) -> p kc f", f=512),
                          dsc["d_rms2T"][:, tc * 512:(tc + 1) * 512]
                          .rearrange("(kc p) f -> p kc f", p=128),
                          waits=(("te", em.cnt["te"]), ("ve", em.cnt["ve"])), ch="x")
            ch1 = ve(lambda e: e.tensor_copy(r2hi, rms2tc),
                     waits=(("dm_x", ldr2), ("te", em.cnt["te"])))
            ch2 = ve(lambda e: e.scalar_tensor_tensor(r2lo, rms2tc, 1.0, r2hi,
                                                      ALU.mult, ALU.subtract))
            for e_ in range(4):
                ldc = em.dma("sync", Ee[0:1, 1024:1536],
                             dsc["d_comb"][e_:e_ + 1, tc * 512:(tc + 1) * 512],
                             waits=(("te", em.cnt["te"]),), ch="t")
                mo = te(lambda e, e_=e_, tc=tc: e.matmul(
                    ps[:, 3072:3584], cOnes[:],
                    Ee[0:1, 1024:1536], start=True, stop=True),
                    waits=use_bank(6, (("ve", em.cnt["ve"]), ("dm_t", ldc))))
                cbv = ve(lambda e, e_=e_: e.tensor_copy(cbc[:, e_ * 512:(e_ + 1) * 512],
                                                        ps[:, 3072:3584]),
                         waits=(("te", mo),))
                read_bank(6, ("ve", cbv))
            for e_ in range(4):
                mm_done = {0: 0, 1: 0}
                for ic in range(NHC):
                    par = ic % 2
                    lds = []
                    for mat, dwn in (("w1", "d_w1"), ("w3", "d_w3")):
                        for part in ("hi", "lo"):
                            lds.append(em.dma(
                                "sync",
                                w13slab(par, mat, part).rearrange("p (kc j) -> p kc j", j=128),
                                dsc[dwn + part][e_][:, ic * 128:(ic + 1) * 128]
                                .rearrange("(kc p) j -> p kc j", p=128),
                                waits=(("te", mm_done[par]),),
                                ch=("w" if mat == "w1" else "h")))
                    ldw = lds[1]; ldu = lds[3]
                    mg = mu = None
                    for pa, pb in (("hi", "hi"), ("hi", "lo"), ("lo", "hi")):
                        for kc in range(8):
                            rr = r2hi if pb == "hi" else r2lo
                            first = (pa == "hi" and pb == "hi" and kc == 0)
                            mg = te(lambda e, kc=kc, par=par, pa=pa, rr=rr, first=first, last=(pa == "lo" and kc == 7): e.matmul(
                                ps[:, 0:512], w13slab(par, "w1", pa)[:, kc * 128:(kc + 1) * 128],
                                rr[:, kc * 512:(kc + 1) * 512],
                                start=first, stop=last),
                                waits=use_bank(0, (("dm_w", ldw), ("dm_x", ldr2),
                                                   ("ve", ch2))) if first else ())
                    for pa, pb in (("hi", "hi"), ("hi", "lo"), ("lo", "hi")):
                        for kc in range(8):
                            rr = r2hi if pb == "hi" else r2lo
                            first = (pa == "hi" and pb == "hi" and kc == 0)
                            mu = te(lambda e, kc=kc, par=par, pa=pa, rr=rr, first=first, last=(pa == "lo" and kc == 7): e.matmul(
                                ps[:, 512:1024], w13slab(par, "w3", pa)[:, kc * 128:(kc + 1) * 128],
                                rr[:, kc * 512:(kc + 1) * 512],
                                start=first, stop=last),
                                waits=use_bank(1, (("dm_h", ldu),)) if first else ())
                    mm_done[par] = mu
                    c_si = se(lambda e: e.activation(silu_sb, ps[:, 0:512], AF.Silu,
                                                     bias=0.0, scale=1.0),
                              waits=(("te", mg), ("ve", em.cnt["ve"])))
                    read_bank(0, ("se", c_si))
                    c_us = ve(lambda e, e_=e_: e.scalar_tensor_tensor(
                        usc, ps[:, 512:1024], 1.0, cbc[:, e_ * 512:(e_ + 1) * 512],
                        ALU.mult, ALU.mult), waits=(("te", mu),))
                    read_bank(1, ("ve", c_us))
                    c_h = ve(lambda e: e.scalar_tensor_tensor(
                        h32, usc, 1.0, silu_sb, ALU.mult, ALU.mult),
                        waits=(("se", c_si),))
                    chh = ve(lambda e, ic=ic: e.tensor_copy(
                        h_hi[:, ic * 512:(ic + 1) * 512], h32))
                    chl = ve(lambda e, ic=ic: e.scalar_tensor_tensor(
                        h_lo[:, ic * 512:(ic + 1) * 512], h32, 1.0,
                        h_hi[:, ic * 512:(ic + 1) * 512], ALU.mult, ALU.subtract))
                w2_done = {0: 0, 1: 0}
                for hc in range(NDC):
                    mo = None
                    for ich in range(2):
                        par = ich
                        ldh2 = []
                        for part in ("hi", "lo"):
                            ldh2.append(em.dma(
                                "sync",
                                w2slab(par, part).rearrange("p (ic j) -> p ic j", j=128),
                                dsc["d_w2" + part][e_][ich * 12 * 128:(ich + 1) * 12 * 128,
                                                       hc * 128:(hc + 1) * 128]
                                .rearrange("(ic p) j -> p ic j", p=128),
                                waits=(("te", w2_done[par]),),
                                ch=("h" if part == "lo" else "w")))
                        for pa, pb in (("hi", "hi"), ("hi", "lo"), ("lo", "hi")):
                            hsrc = h_hi if pb == "hi" else h_lo
                            for ici in range(12):
                                ic = ich * 12 + ici
                                fw = (pa == "hi" and pb == "hi" and ici == 0)
                                mo = te(lambda e, ic=ic, ici=ici, par=par, pa=pa,
                                        hsrc=hsrc,
                                        first=(ich == 0 and pa == "hi" and pb == "hi" and ici == 0),
                                        last=(ich == 1 and pa == "lo" and ici == 11): e.matmul(
                                    ps[:, 1024:1536],
                                    w2slab(par, pa)[:, ici * 128:(ici + 1) * 128],
                                    hsrc[:, ic * 512:(ic + 1) * 512],
                                    start=first, stop=last, skip_group_check=True),
                                    waits=use_bank(2, (("dm_w", ldh2[0]),
                                                       ("dm_h", ldh2[1]),
                                                       ("ve", em.cnt["ve"]))) if fw else ())
                        w2_done[par] = mo
                    if e_ == 0:
                        cacc = ve(lambda e, hc=hc: e.tensor_copy(
                            macc[:, hc * 512:(hc + 1) * 512], ps[:, 1024:1536]),
                            waits=(("te", mo),))
                    else:
                        cacc = ve(lambda e, hc=hc: e.tensor_tensor(
                            macc[:, hc * 512:(hc + 1) * 512],
                            macc[:, hc * 512:(hc + 1) * 512], ps[:, 1024:1536], ALU.add),
                            waits=(("te", mo),))
                    read_bank(2, ("ve", cacc))
            for hc in range(NDC):
                lxm = em.dma("sync", xmT, dsc["d_xmidT"][hc * 128:(hc + 1) * 128,
                                                         tc * 512:(tc + 1) * 512],
                             waits=(("ve", em.cnt["ve"]),), ch="x")
                c_xo = ve(lambda e, hc=hc: e.tensor_tensor(
                    xoT, macc[:, hc * 512:(hc + 1) * 512], xmT, ALU.add),
                    waits=(("dm_x", lxm),))
                c_g = se(lambda e: e.activation(gsil, xoT, AF.Silu, bias=0.0, scale=1.0),
                         waits=(("ve", c_xo),))
                lds = em.dma("sync", sTt, dsc["d_sT"][hc * 128:(hc + 1) * 128,
                                                      tc * 512:(tc + 1) * 512],
                             waits=(("ve", em.cnt["ve"]),), ch="t")
                c_d = ve(lambda e: e.tensor_tensor(dT, gsil, sTt, ALU.mult),
                         waits=(("se", c_g), ("dm_t", lds)))
                for j in range(4):
                    tc_t = te(lambda e, j=j: e.transpose(
                        ps[:, 1536 + j * 128:1536 + (j + 1) * 128],
                        dT[:, j * 128:(j + 1) * 128], cI[:]),
                        waits=use_bank(3, (("ve", c_d),)))
                c_ev = ve(lambda e: e.tensor_copy(
                    devac.rearrange("p (j f) -> p j f", f=128),
                    ps[:, 1536:2048].rearrange("p (j f) -> p j f", f=128)),
                    waits=(("te", em.cnt["te"]), ("dmg", em.cnt["dmg"])))
                read_bank(3, ("ve", c_ev))
                em.do("gpsimd", lambda e, hc=hc, tc=tc: e.dma_start(
                    out=d_y[tc * 512:(tc + 1) * 512, hc * 128:(hc + 1) * 128]
                    .rearrange("(j p) f -> p j f", p=128),
                    in_=devac.rearrange("p (j f) -> p j f", f=128),
                    accum_op=ALU.add),
                    waits=(("ve", c_ev), ("dm_s", xst), ("dmg", em.cnt["dmg"])),
                    sem="dmg", inc=16)
        em.barrier()


    FAKE_ST = os.environ.get("KD_FAKE_ST", "0") == "1"
    if FAKE_ST:
        em.dma("sync", dsc["d_sT"], dsc["d_xnT"],
               waits=(("dm_s", em.cnt["dm_s"]),), ch="s")
        em.barrier()
    if phase_on():
        phase_L0G()

    if phase_on():
        phase_L0A(dsc["d_x"], st_r1, st_mr1, dsc["d_xnT"], smax_col=10)
        em.barrier()
    if phase_on():
        phase_L1B()
    if phase_on():
        emit_gbcast(din["L1_ffn_g1"], l0w[:, 0:1024])
        phase_normT(dsc["d_x"], dsc["d_rms1T"], l0w[:, 0:1024], st_rr1,
                    (st_r1, st_mr1))
    if phase_on():
        phase_L1D()
    if phase_on():
        phase_L1E()
        emit_gbcast(din["L1_ffn_g2"], l0w[:, 1024:2048])
        phase_normT(dsc["d_xmid"], dsc["d_rms2T"], l0w[:, 1024:2048], st_rr2, None)
    if phase_on():
        phase_L1F()

    if d_stats is not None:
        em.dma("sync", d_stats[:, 0:16], st_r0[:],
               waits=(("ve", em.cnt["ve"]),), ch="o")
        em.dma("sync", d_stats[:, 16:32], st_mr0[:], ch="o")
        em.dma("sync", d_stats[:, 32:48], sm[:, 0:16], ch="o")
    for n, ap in dbg.items():
        em.dma("sync", ap, dsc[n],
               waits=tuple([("ve", em.cnt["ve"]), ("te", em.cnt["te"]),
                            ("se", em.cnt["se"]), ("dmg", em.cnt["dmg"])] +
                           [(s, em.cnt[s]) for s in em.cnt
                            if s.startswith("dm_") and em.cnt[s] > 0]),
               ch="o")
    em.do("sync", lambda e: e.nop(),
          waits=tuple([("dmg", em.cnt["dmg"])] +
                      [(s, em.cnt[s]) for s in em.cnt
                       if s.startswith("dm_") and em.cnt[s] > 0]))

    em.emit()
    stack.close()
    return nc, cns


def kernel(**inputs):
    debug_names = tuple(os.environ.get("KD_DUMP", "").split(",")) if os.environ.get("KD_DUMP") else ()
    nc, cns = build(debug_names)
    x = np.ascontiguousarray(np.asarray(inputs["L0_attn_x"], dtype=np.float32))
    in_maps = []
    for b in range(8):
        m = {}
        for k, v in inputs.items():
            a = np.asarray(v, dtype=np.float32)
            if k == "L0_attn_x":
                a = x[b]
            elif k in ("L0_ffn_clip_x", "L0_ffn_clip_down"):
                a = a.reshape(1, 1)
            elif k in ("L1_ffn_g1", "L1_ffn_g2"):
                a = a.reshape(1, D)
            m[k] = np.ascontiguousarray(a)
        for k, v in cns.items():
            m[k] = v
        in_maps.append(m)
    res = run_bass_kernel_spmd(nc, in_maps, core_ids=list(range(8)))
    out = np.stack([np.asarray(r["y"]) for r in res.results], axis=0)
    kernel.last_results = res.results
    return out


if __name__ == "__main__":
    rng = np.random.default_rng(0)
    ins = {"L0_attn_x": rng.standard_normal((8, T, D)).astype(np.float32)}
    for n, s in [("L0_attn_W_dq", (128, D)), ("L0_attn_W_uq", (D, 128)),
                 ("L0_attn_W_dkv", (128, D)), ("L0_attn_W_uk", (D, 128)),
                 ("L0_attn_W_uv", (D, 128)), ("L0_attn_W_o", (D, D)),
                 ("L0_ffn_w_gate", (I0, D)), ("L0_ffn_w_up", (I0, D)),
                 ("L0_ffn_w_down", (D, I0)), ("L1_attn_Wq", (D, D)),
                 ("L1_attn_Wkv", (D, 512)), ("L1_attn_Wo", (D, D)),
                 ("L1_ffn_wq", (D, D)), ("L1_ffn_wk", (D, 256)), ("L1_ffn_wv", (D, 256)),
                 ("L1_ffn_wo", (D, D)), ("L1_ffn_gate", (D, 4)),
                 ("L1_ffn_w1", (E_MOE, D, HID)), ("L1_ffn_w2", (E_MOE, HID, D)),
                 ("L1_ffn_w3", (E_MOE, D, HID))]:
        ins[n] = (rng.standard_normal(s) * 0.02).astype(np.float32)
    ins["L0_ffn_clip_x"] = np.full((1,), 4.0, np.float32)
    ins["L0_ffn_clip_down"] = np.full((1,), 4.0, np.float32)
    ins["L1_ffn_g1"] = np.ones((D,), np.float32)
    ins["L1_ffn_g2"] = np.ones((D,), np.float32)
    out = kernel(**ins)
    print("ran, out shape", out.shape)
